# version 05
# baseline (speedup 1.0000x reference)
import numpy as np
import ml_dtypes
from contextlib import ExitStack

import concourse.bass as bass
import concourse.mybir as mybir
from concourse.bass_utils import run_bass_kernel_spmd

F32 = mybir.dt.float32
BF16 = mybir.dt.bfloat16
I32 = mybir.dt.int32
AF = mybir.ActivationFunctionType
OP = mybir.AluOpType

B, S, D = 2, 4096, 2048
E3, HID = 3 * D, 5632
ROT = D // 2
TPC = 1024
NT = 512
NCD = D // 128
NCH = HID // 128
EPS = 1e-5
INVD = 1.0 / D
N_CORES = 8

ENGS = ("sync", "gpsimd", "tensor", "vector", "scalar")
_LAST_PROG = None


class Prog:
    def __init__(self, nc, ctx):
        self.nc = nc
        self.ctx = ctx
        self.ops = {k: [] for k in ENGS}
        self.sems = {}
        self.cnt = {}
        for name in ("ld", "ar", "cc", "pe", "dv", "sc", "st"):
            self.new_sem(name)
        for i in range(2):
            self.new_sem(f"wa{i}")
        for i in range(2):
            self.new_sem(f"wb{i}")
            self.new_sem(f"wd{i}")
        for i in range(6):
            self.new_sem(f"tab{i}")
        self.free_ev = {}
        self.trace = {k: [] for k in ENGS}

    def new_sem(self, name):
        self.sems[name] = self.ctx.enter_context(self.nc.semaphore("s_" + name))
        self.cnt[name] = 0

    def emit(self, eng, fn):
        self.ops[eng].append(fn)

    def wait(self, eng, sem, val):
        if val is None or val <= 0:
            return
        s = self.sems[sem]
        self.trace[eng].append(("w", sem, val))
        self.emit(eng, lambda e: e.wait_ge(s, val))

    def wait_ev(self, eng, ev):
        if ev is not None:
            self.wait(eng, ev[0], ev[1])

    def sig(self, eng, buildfn, sem, k=1):
        s = self.sems[sem]
        self.cnt[sem] += k
        v = self.cnt[sem]
        self.trace[eng].append(("s", sem, k))
        self.emit(eng, lambda e: buildfn(e).then_inc(s, k))
        return v

    def run(self, eng, buildfn):
        self.emit(eng, lambda e: buildfn(e))


def build_nc(detect_races=True):
    nc = bass.Bass("TRN2", target_bir_lowering=False, debug=False,
                   detect_race_conditions=detect_races)

    dram = nc.dram_tensor
    xT = dram("xT", [D, TPC], F32, kind="ExternalInput").ap()
    w_in_r = dram("w_in_r", [48, 128, NCD * 128], F32, kind="ExternalInput").ap()
    w_out_r = dram("w_out_r", [16, 128, NCD * 128], F32, kind="ExternalInput").ap()
    wqg_r = dram("wqg_r", [44, 128, NCD * 128], I32, kind="ExternalInput").ap()
    wqu_r = dram("wqu_r", [44, 128, NCD * 128], I32, kind="ExternalInput").ap()
    wqd_r = dram("wqd_r", [16, 128, NCH * 128], I32, kind="ExternalInput").ap()
    cosT = dram("cosT", [ROT, TPC], BF16, kind="ExternalInput").ap()
    sinT = dram("sinT", [ROT, TPC], BF16, kind="ExternalInput").ap()
    b_in_pp = dram("b_in_pp", [128, 48], F32, kind="ExternalInput").ap()
    b_out_pp = dram("b_out_pp", [128, 16], F32, kind="ExternalInput").ap()
    sg_pp = dram("sg_pp", [128, 44], F32, kind="ExternalInput").ap()
    su_pp = dram("su_pp", [128, 44], F32, kind="ExternalInput").ap()
    sd_pp = dram("sd_pp", [128, 16], F32, kind="ExternalInput").ap()
    ones_r = dram("ones_r", [1, 128], BF16, kind="ExternalInput").ap()
    ones_c = dram("ones_c", [128, 1], BF16, kind="ExternalInput").ap()
    perm = dram("perm", [128, 128], BF16, kind="ExternalInput").ap()
    yT = dram("yT", [D, TPC], F32, kind="ExternalOutput").ap()
    ar_in = dram("ar_in", [128, 16], F32)
    ar_out = dram("ar_out", [128, 16], F32)

    ctx = ExitStack()
    with ctx:
        alloc = nc.alloc_sbuf_tensor
        x_sb = alloc("x_sb", [128, NCD * TPC], F32)
        xn_sb = alloc("xn_sb", [128, NCD * TPC], BF16)
        scratch = alloc("scratch", [128, 98304], mybir.dt.uint8)
        sbase = nc.lookup_mloc(scratch).addr
        at = nc.alloc_sbuf_tensor_at
        q_sb = at("q_sb", [128, NCD * TPC], BF16, offset=sbase + 0)
        k_sb = at("k_sb", [128, NCD * TPC], BF16, offset=sbase + 32768)
        ao_sb = at("ao_sb", [128, NCD * TPC], BF16, offset=sbase + 32768)
        v_sb = at("v_sb", [128, 2 * NT], BF16, offset=sbase + 65536)
        tab_sb = at("tab_sb", [128, 6 * NT], BF16, offset=sbase + 67584)
        wA_sb = at("wA_sb", [128, 2 * NCD * 128], BF16, offset=sbase + 73728)
        h_sb = at("h_sb", [128, NCH * TPC], BF16, offset=sbase + 0)
        wB_sb = at("wB_sb", [128, 2 * NCD * 128], BF16, offset=sbase + 90112)
        wD_sb = at("wD_sb", [128, 2 * 11 * 128], BF16, offset=sbase + 90112)

        junk = alloc("junk", [128, 3 * NT], BF16)
        g_tmp = alloc("g_tmp", [128, 2 * NT], BF16)
        u_tmp = alloc("u_tmp", [128, 2 * NT], BF16)
        row_f = alloc("row_f", [1, 1 * NT], F32)
        row_b = alloc("row_b", [1, 4 * NT], BF16)
        A_acc = alloc("A_acc", [128, 32], F32)
        A_sum = alloc("A_sum", [128, 16], F32)
        aro_sb = alloc("aro_sb", [128, 16], F32)
        bin_sb = alloc("bin_sb", [128, 48], F32)
        bout_sb = alloc("bout_sb", [128, 16], F32)
        sg_sb = alloc("sg_sb", [128, 44], F32)
        su_sb = alloc("su_sb", [128, 44], F32)
        sd_sb = alloc("sd_sb", [128, 16], F32)
        onr_sb = alloc("onr_sb", [1, 128], BF16)
        onc_sb = alloc("onc_sb", [128, 1], BF16)
        perm_sb = alloc("perm_sb", [128, 128], BF16)

        pa = nc.alloc_psum_tensor
        p_acc = [pa("p_acc0", [128, NT], F32), pa("p_acc1", [128, NT], F32)]
        p_rh = [pa("p_rh0", [128, NT], F32), pa("p_rh1", [128, NT], F32)]
        p_sum = pa("p_sum", [128, NT], F32)
        p_sq = pa("p_sq", [128, NT], F32)
        p_bc = pa("p_bc", [128, NT], F32)
        p_inv = pa("p_inv", [128, NT], F32)

        BK_SUM = ("p_sum", None)
        BK_SQ = ("p_sq", None)
        P = Prog(nc, ctx)
        global _LAST_PROG
        _LAST_PROG = P
        free = P.free_ev

        def tok(c, tau):
            o = c * TPC + tau * NT
            return slice(o, o + NT)

        def atk(t, c, tau):
            return t[:, tok(c, tau)]

        jslot = lambda i: junk[:, i * NT:(i + 1) * NT]
        r_f = lambda i: row_f[:, i * NT:(i + 1) * NT]
        r_b = lambda i: row_b[:, i * NT:(i + 1) * NT]
        vslot = lambda i: v_sb[:, (i % 2) * NT:((i % 2) + 1) * NT]
        tslot = lambda i: tab_sb[:, (i % 6) * NT:((i % 6) + 1) * NT]
        wA_slot = lambda i: wA_sb[:, (i % 2) * 2048:((i % 2) + 1) * 2048]
        wA_tile = lambda i, c: wA_sb[:, (i % 2) * 2048 + c * 128:(i % 2) * 2048 + (c + 1) * 128]
        wB_slot = lambda i: wB_sb[:, (i % 2) * 2048:((i % 2) + 1) * 2048]
        wB_tile = lambda i, c: wB_sb[:, (i % 2) * 2048 + c * 128:(i % 2) * 2048 + (c + 1) * 128]
        wD_slot = lambda i: wD_sb[:, (i % 2) * 1408:((i % 2) + 1) * 1408]
        wD_tile = lambda i, c: wD_sb[:, (i % 2) * 1408 + c * 128:(i % 2) * 1408 + (c + 1) * 128]

        for dst, src in ((bin_sb, b_in_pp), (bout_sb, b_out_pp), (sg_sb, sg_pp),
                         (su_sb, su_pp), (sd_sb, sd_pp), (onr_sb, ones_r),
                         (onc_sb, ones_c), (perm_sb, perm)):
            P.sig("sync", lambda e, d=dst, s=src: e.dma_start(out=d[:], in_=s), "ld", 16)
        ev_cst = P.cnt["ld"]
        x3 = xT.rearrange("(c p) t -> p c t", p=128)
        x_sb3 = x_sb[:].rearrange("p (c t) -> p c t", c=NCD)
        ev_xq = []
        for q in range(4):
            ev_xq.append(P.sig("sync", lambda e, q=q: e.dma_start(
                out=x_sb3[:, 4 * q:4 * (q + 1), :], in_=x3[:, 4 * q:4 * (q + 1), :]),
                "ld", 16))
        for eng in ("tensor", "vector", "scalar"):
            P.wait(eng, "ld", ev_cst)

        win_order = list(range(16, 48)) + list(range(16))
        wa_loads = []
        wa_user_ev = {}
        wa_state = [0]

        def emit_wa_load():
            i = wa_state[0]
            if i >= 64:
                return
            if i >= 2:
                P.wait_ev("gpsimd", wa_user_ev.get(i - 2))
            src = w_in_r[win_order[i]] if i < 48 else w_out_r[i - 48]
            sem = f"wa{i % 2}"
            v = P.sig("gpsimd", lambda e, s=src, i=i: e.dma_start(out=wA_slot(i), in_=s),
                      sem, 16)
            wa_loads.append((sem, v))
            wa_state[0] += 1

        for _ in range(2):
            emit_wa_load()

        bank_free = {}

        def main_group(tiles, rhs, bank, extra_k1=None):
            P.wait_ev("tensor", bank_free.get(bank.name))
            items = list(zip(tiles, rhs))
            nk1 = len(extra_k1) if extra_k1 else 0
            tot = len(items) + nk1
            ev = None
            for i, (lt, rs) in enumerate(items):
                last = (i == tot - 1)
                fn = lambda e, lt=lt, rs=rs, st=(i == 0), sp=last: e.matmul(
                    bank.ap(), lt, rs, start=st, stop=sp)
                if last:
                    ev = P.sig("tensor", fn, "pe")
                else:
                    P.run("tensor", fn)
            if extra_k1:
                for j, (lt, rs) in enumerate(extra_k1):
                    last = (len(items) + j == tot - 1)
                    fn = lambda e, lt=lt, rs=rs, sp=last: e.matmul(
                        bank.ap(), lt, rs, start=False, stop=sp)
                    if last:
                        ev = P.sig("tensor", fn, "pe")
                    else:
                        P.run("tensor", fn)
            return ev

        def emit_stats_resid(tau, src_evs=None, skip_sum=False, xn_guard=None,
                             src_sc_ev=None):
            P.wait_ev("scalar", src_sc_ev)
            if not skip_sum:
                P.wait_ev("tensor", bank_free.get("p_sum"))
            P.wait_ev("tensor", bank_free.get("p_sq"))
            ev_sum = ev_sq = None
            for c in range(NCD):
                last = c == NCD - 1
                if src_evs is not None:
                    P.wait("vector", "ld", src_evs[c // 4])
                    P.wait("scalar", "ld", src_evs[c // 4])
                if xn_guard is not None and c == 0:
                    P.wait_ev("vector", xn_guard)
                ev_c = P.sig("vector", lambda e, c=c, tau=tau: e.tensor_copy(
                    atk(xn_sb, c, tau), atk(x_sb, c, tau)), "dv")
                P.wait_ev("scalar", free.get(("j", 2)))
                ev_s = P.sig("scalar", lambda e, c=c, tau=tau: e.activation(
                    jslot(2), atk(x_sb, c, tau), AF.Square), "sc")
                if not skip_sum:
                    P.wait("tensor", "dv", ev_c)
                    fn_sum = lambda e, c=c, tau=tau, st=(c == 0), sp=last: e.matmul(
                        p_sum[0:1, :], onc_sb[:], atk(xn_sb, c, tau), start=st, stop=sp)
                    if last:
                        ev_sum = P.sig("tensor", fn_sum, "pe")
                    else:
                        P.run("tensor", fn_sum)
                P.wait("tensor", "sc", ev_s)
                ev_sq = P.sig("tensor", lambda e, st=(c == 0), sp=last: e.matmul(
                    p_sq[0:1, :], onc_sb[:], jslot(2), start=st, stop=sp), "pe")
                free[("j", 2)] = ("pe", ev_sq)
            return ev_sum, ev_sq

        mu_save = lambda tau: v_sb[0:1, tau * NT:(tau + 1) * NT]
        bcA = lambda: g_tmp[:, 0:NT]
        bcB = lambda: g_tmp[:, NT:2 * NT]

        def emit_ln_tail(sum_bk, sq_bk, ev_sum, ev_sq, src_f32, src_bf, tau,
                         resid, on_chunk=None, save_mu=False, mu_from=None):
            sum_name, sum_row = sum_bk
            sq_name, sq_row = sq_bk
            if mu_from is None:
                P.wait("vector", "pe", ev_sum)
                P.sig("vector", lambda e: e.tensor_scalar(
                    r_b(3), sum_row, INVD, None, OP.mult), "dv")
                bank_free[sum_name] = ("dv", P.cnt["dv"])
                if save_mu:
                    P.run("vector", lambda e: e.tensor_copy(mu_save(tau), r_b(3)))
                mu_row = r_b(3)
            else:
                mu_row = mu_from
            P.wait("vector", "pe", ev_sq)
            P.sig("vector", lambda e: e.tensor_scalar(
                r_f(0), sq_row, INVD, EPS, OP.mult, OP.add), "dv")
            bank_free[sq_name] = ("dv", P.cnt["dv"])
            P.run("vector", lambda e, m=mu_row: e.tensor_mul(r_b(2), m, m))
            ev_var = P.sig("vector", lambda e: e.tensor_sub(r_f(0), r_f(0), r_b(2)), "dv")
            P.wait("scalar", "dv", ev_var)
            P.run("scalar", lambda e: e.activation(r_f(0), r_f(0), AF.Ln))
            ev_rstd = P.sig("scalar", lambda e: e.activation(
                r_f(0), r_f(0), AF.Exp, scale=-0.5), "sc")
            P.wait("vector", "sc", ev_rstd)
            P.run("vector", lambda e: e.tensor_copy(r_b(0), r_f(0)))
            ev_rows = P.sig("vector", lambda e, m=mu_row: e.scalar_tensor_tensor(
                r_b(1), m, -1.0, r_f(0), OP.mult, OP.mult), "dv")
            P.wait_ev("tensor", bank_free.get("p_bc"))
            P.wait_ev("tensor", bank_free.get("p_inv"))
            P.wait("tensor", "dv", ev_rows)
            P.run("tensor", lambda e: e.matmul(p_bc.ap(), onr_sb[:], r_b(0),
                                               start=True, stop=True))
            ev_bc = P.sig("tensor", lambda e: e.matmul(
                p_inv.ap(), onr_sb[:], r_b(1), start=True, stop=True), "pe")
            P.wait("vector", "pe", ev_bc)
            P.wait_ev("vector", free.get(("g", 0)))
            P.run("vector", lambda e: e.tensor_copy(bcA(), p_bc.ap()))
            P.wait_ev("vector", free.get(("g", 1)))
            ev_bce = P.sig("vector", lambda e: e.tensor_copy(bcB(), p_inv.ap()), "dv")
            bank_free["p_bc"] = ("dv", ev_bce)
            bank_free["p_inv"] = ("dv", ev_bce)
            ev_ap = None
            for c in range(NCD):
                P.wait_ev("vector", free.get(("j", 0)))
                if not resid:
                    P.run("vector", lambda e, c=c, tau=tau: e.tensor_tensor(
                        jslot(0), atk(xn_sb, c, tau), bcA(), OP.mult))
                    ev_ap = P.sig("vector", lambda e, c=c, tau=tau: e.tensor_tensor(
                        atk(xn_sb, c, tau), jslot(0), bcB(), OP.add), "dv")
                else:
                    P.wait_ev("vector", free.get(("j", 1)))
                    P.run("vector", lambda e, c=c, tau=tau: e.tensor_tensor(
                        jslot(0), atk(src_bf, c, tau), bcA(), OP.mult))
                    P.run("vector", lambda e, c=c: e.tensor_tensor(
                        jslot(1), jslot(0), bcB(), OP.add))
                    ev_ap = P.sig("vector", lambda e, c=c, tau=tau: e.tensor_tensor(
                        atk(x_sb, c, tau), atk(x_sb, c, tau), jslot(1), OP.add), "dv")
                    free[("j", 1)] = ("dv", ev_ap)
                free[("j", 0)] = ("dv", ev_ap)
                if on_chunk is not None:
                    on_chunk(c, ev_ap)
            free[("g", 0)] = ("dv", ev_ap)
            free[("g", 1)] = ("dv", ev_ap)
            return ev_ap

        def make_ln_sink(src_bf, sum_banks, sq_banks):
            st = {"sum_ev": {}, "sq_ev": {}}

            def feed(c, tau, ev_src):
                if c == 0:
                    P.wait_ev("tensor", bank_free.get(sum_banks[tau][0]))
                    P.wait_ev("tensor", bank_free.get(sq_banks[tau][0]))
                P.wait_ev("scalar", ev_src)
                P.wait_ev("scalar", free.get(("j", 2)))
                ev_s = P.sig("scalar", lambda e, c=c, tau=tau: e.activation(
                    jslot(2), atk(src_bf, c, tau), AF.Square), "sc")
                last = c == NCD - 1
                P.wait_ev("tensor", ev_src)
                fn_sum = lambda e, c=c, tau=tau, st=(c == 0), sp=last: e.matmul(
                    sum_banks[tau][1], onc_sb[:], atk(src_bf, c, tau),
                    start=st, stop=sp)
                if last:
                    st["sum_ev"][tau] = P.sig("tensor", fn_sum, "pe")
                else:
                    P.run("tensor", fn_sum)
                P.wait("tensor", "sc", ev_s)
                ev_q = P.sig("tensor", lambda e, tau=tau, st=(c == 0), sp=last: e.matmul(
                    sq_banks[tau][1], onc_sb[:], jslot(2), start=st, stop=sp), "pe")
                free[("j", 2)] = ("pe", ev_q)
                if last:
                    st["sq_ev"][tau] = ev_q
            return feed, st

        tab_state = [0]
        tab_loads = {}
        tabq = []

        def emit_tab_load(which, rc, tau):
            i = tab_state[0]
            P.wait_ev("sync", free.get(("tab", i % 6)))
            src = (cosT if which == 0 else sinT)[rc * 128:(rc + 1) * 128,
                                                 tau * NT:(tau + 1) * NT]
            sem = f"tab{i % 6}"
            v = P.sig("sync", lambda e, s=src, i=i: e.dma_start(out=tslot(i), in_=s),
                      sem, 16)
            tab_loads[i] = (sem, v)
            tab_state[0] += 1
            return i

        def prefetch_tabs(rc):
            for tau in (0, 1):
                a = emit_tab_load(0, rc, tau)
                b = emit_tab_load(1, rc, tau)
                tabq.append((a, b))

        rope_i = [0]

        def emit_rope(slice_, ev_evac):
            i = rope_i[0]
            rope_i[0] += 1
            bank = p_rh[i % 2]
            P.wait_ev("tensor", bank_free.get(bank.name))
            P.wait("tensor", "sc", ev_evac)
            ev_rh = P.sig("tensor", lambda e: e.matmul(
                bank.ap(), perm_sb[:], slice_, start=True, stop=True), "pe")
            slc, sls = tabq.pop(0)
            P.wait_ev("vector", tab_loads[slc])
            P.wait_ev("vector", tab_loads[sls])
            P.wait("vector", "sc", ev_evac)
            P.wait("vector", "pe", ev_rh)
            P.wait_ev("vector", free.get(("j", 2)))
            P.run("vector", lambda e: e.tensor_mul(jslot(2), bank.ap(), tslot(sls)))
            P.run("vector", lambda e: e.tensor_mul(slice_, slice_, tslot(slc)))
            ev = P.sig("vector", lambda e: e.tensor_add(slice_, slice_, jslot(2)), "dv")
            bank_free[bank.name] = ("dv", ev)
            for s in (slc, sls):
                free[("tab", s % 6)] = ("dv", ev)
            free[("j", 2)] = ("dv", ev)
            return ev

        ev_xn_t = {}
        for tau in (0, 1):
            ev_sum, ev_sq = emit_stats_resid(tau, src_evs=ev_xq)
            ev_xn_t[tau] = emit_ln_tail(("p_sum", p_sum[0:1, :]), ("p_sq", p_sq[0:1, :]),
                                        ev_sum, ev_sq, x_sb, None, tau, resid=False,
                                        save_mu=True)
        prefetch_tabs(0)

        gpar = [0]

        def rot(banks):
            st = {"i": -1}
            def nxt():
                st["i"] += 1
                return banks[st["i"] % len(banks)]
            return nxt

        rotA = rot([p_acc[0], p_acc[1], p_sum, p_bc])

        def qkv_group(li, tau, rhs_src=None, extra_wait=None):
            bank = rotA()
            P.wait_ev("tensor", wa_loads[li])
            if extra_wait is not None:
                P.wait_ev("tensor", extra_wait)
            tiles = [wA_tile(li, c) for c in range(NCD)]
            src = rhs_src if rhs_src is not None else xn_sb
            rhs = [atk(src, c, tau) for c in range(NCD)]
            return bank, main_group(tiles, rhs, bank)

        k_done = {}
        for li, e_ch in enumerate(win_order[:16]):
            kc = e_ch - 16
            for tau in (0, 1):
                ew = ("dv", ev_xn_t[tau]) if li == 0 else None
                bank, ev_g = qkv_group(li, tau, extra_wait=ew)
                if tau == 1:
                    wa_user_ev[li] = ("pe", ev_g)
                P.wait("scalar", "pe", ev_g)
                dst = atk(k_sb, kc, tau)
                ev_evac = P.sig("scalar", lambda e, d=dst, ec=e_ch, bank=bank: e.activation(
                    d, bank.ap(), AF.Identity, bias=bin_sb[:, ec:ec + 1]), "sc")
                bank_free[bank.name] = ("sc", ev_evac)
                if kc < 8:
                    k_done[(kc, tau)] = ("dv", emit_rope(dst, ev_evac))
                else:
                    k_done[(kc, tau)] = ("sc", ev_evac)
            if kc < 8:
                if kc + 1 < 8:
                    prefetch_tabs(kc + 1)
                else:
                    prefetch_tabs(0)
            emit_wa_load()

        fmk_done = {}
        fm_sq_last = {}

        fm_banks = {0: ("p_sq", lambda: p_sq[0:1, :]), 1: ("p_rh0", lambda: p_rh[0][0:1, :])}

        def fm_feed(c, tau):
            s, v = k_done[(c, tau)]
            P.wait("scalar", s, v)
            P.wait_ev("scalar", free.get(("j", 2)))
            ev_s = P.sig("scalar", lambda e, c=c, tau=tau: e.activation(
                jslot(2), atk(k_sb, c, tau), AF.Square), "sc")
            bname, brow = fm_banks[tau]
            if c == 0:
                P.wait_ev("tensor", bank_free.get(bname))
            P.wait("tensor", "sc", ev_s)
            ev = P.sig("tensor", lambda e, brow=brow, st=(c == 0), sp=(c == NCD - 1): e.matmul(
                brow(), onc_sb[:], jslot(2), start=st, stop=sp), "pe")
            free[("j", 2)] = ("pe", ev)
            if c == NCD - 1:
                fm_sq_last[tau] = ev

        fm_bc_banks = {0: ("p_inv", lambda: p_inv.ap()), 1: ("p_rh1", lambda: p_rh[1].ap())}

        def fm_rows_bcast(tau):
            sname, srow = fm_banks[tau]
            P.wait("vector", "pe", fm_sq_last[tau])
            ev_s0 = P.sig("vector", lambda e, srow=srow: e.tensor_scalar_max(
                r_f(0), srow(), 1e-24), "dv")
            bank_free[sname] = ("dv", ev_s0)
            P.wait("scalar", "dv", ev_s0)
            P.run("scalar", lambda e: e.activation(r_f(0), r_f(0), AF.Ln))
            ev_e = P.sig("scalar", lambda e: e.activation(
                r_f(0), r_f(0), AF.Exp, scale=-0.5), "sc")
            P.wait("vector", "sc", ev_e)
            P.run("vector", lambda e: e.tensor_scalar_min(r_f(0), r_f(0), 1e12))
            ev_inv = P.sig("vector", lambda e: e.tensor_copy(r_b(0), r_f(0)), "dv")
            bname, bap = fm_bc_banks[tau]
            P.wait_ev("tensor", bank_free.get(bname))
            P.wait("tensor", "dv", ev_inv)
            return P.sig("tensor", lambda e, bap=bap: e.matmul(
                bap(), onr_sb[:], r_b(0), start=True, stop=True), "pe")

        def fmk_mul(c, tau, ev_bci):
            if ev_bci is not None:
                P.wait("vector", "pe", ev_bci)
            bap = fm_bc_banks[tau][1]
            fmk_done[(c, tau)] = P.sig("vector", lambda e, c=c, tau=tau, bap=bap:
                                       e.tensor_tensor(atk(k_sb, c, tau),
                                                       atk(k_sb, c, tau),
                                                       bap(), OP.mult), "dv")

        v_ready = {}
        for li2, e_ch in enumerate(win_order[16:32]):
            li = 16 + li2
            vc = e_ch - 32
            for tau in (0, 1):
                bank, ev_g = qkv_group(li, tau)
                if tau == 1:
                    wa_user_ev[li] = ("pe", ev_g)
                P.wait("scalar", "pe", ev_g)
                dst = atk(q_sb, vc, tau)
                ev_evac = P.sig("scalar", lambda e, d=dst, ec=e_ch, bank=bank: e.activation(
                    d, bank.ap(), AF.Identity, bias=bin_sb[:, ec:ec + 1]), "sc")
                bank_free[bank.name] = ("sc", ev_evac)
                v_ready[(vc, tau)] = ("sc", ev_evac)
            emit_wa_load()
            if li2 < 8:
                fm_feed(2 * li2, 0)
                fm_feed(2 * li2 + 1, 0)
                fm_feed(2 * li2, 1)
                fm_feed(2 * li2 + 1, 1)
            if li2 == 8:
                ev_bci0 = fm_rows_bcast(0)
                for c in range(NCD):
                    fmk_mul(c, 0, ev_bci0 if c == 0 else None)
                bank_free["p_inv"] = ("dv", fmk_done[(NCD - 1, 0)])
            if li2 == 9:
                ev_bci1 = fm_rows_bcast(1)
                for c in range(NCD):
                    fmk_mul(c, 1, ev_bci1 if c == 0 else None)
                bank_free["p_rh1"] = ("dv", fmk_done[(NCD - 1, 1)])

        aslot_free = {}

        def emit_astt(c, tau):
            s, v = v_ready[(c, tau)]
            P.wait("vector", s, v)
            P.wait("vector", "dv", fmk_done[(c, tau)])
            P.wait_ev("vector", free.get(("j", 0)))
            ev_a = P.sig("vector", lambda e, c=c, tau=tau: e.scalar_tensor_tensor(
                jslot(0), atk(q_sb, c, tau), 1.0, atk(k_sb, c, tau),
                OP.mult, OP.mult,
                accum_out=A_acc[:, tau * 16 + c:tau * 16 + c + 1]), "dv")
            free[("j", 0)] = ("dv", ev_a)
            aslot_free[(c, tau)] = ("dv", ev_a)

        for c in range(NCD):
            emit_astt(c, 0)
            emit_astt(c, 1)

        ev_asum = P.sig("vector", lambda e: e.tensor_add(
            A_sum[:], A_acc[:, 0:16], A_acc[:, 16:32]), "dv")
        P.wait("sync", "dv", ev_asum)
        ev_arin = P.sig("sync", lambda e: e.dma_start(out=ar_in.ap(), in_=A_sum[:]), "ar", 16)

        q_done = {}
        for li3, e_ch in enumerate(win_order[32:48]):
            li = 32 + li3
            qc = e_ch
            if li3 == 1:
                P.wait("gpsimd", "ar", ev_arin)
                P.sig("gpsimd", lambda e: e.collective_compute(
                    "AllReduce", OP.add,
                    replica_groups=[[0, 1, 2, 3], [4, 5, 6, 7]],
                    ins=[ar_in.ap().opt()],
                    outs=[ar_out.ap().opt()],
                ), "cc", 1)
            if li3 == 2:
                P.wait("sync", "cc", 1)
                ev_aro = P.sig("sync", lambda e: e.dma_start(
                    out=aro_sb[:], in_=ar_out.ap()), "ar", 16)
            for tau in (0, 1):
                bank, ev_g = qkv_group(li, tau)
                if tau == 1:
                    wa_user_ev[li] = ("pe", ev_g)
                P.wait("scalar", "pe", ev_g)
                P.wait_ev("scalar", aslot_free[(qc, tau)])
                dst = atk(q_sb, qc, tau)
                ev_evac = P.sig("scalar", lambda e, d=dst, ec=e_ch, bank=bank: e.activation(
                    d, bank.ap(), AF.Identity, bias=bin_sb[:, ec:ec + 1]), "sc")
                bank_free[bank.name] = ("sc", ev_evac)
                if qc < 8:
                    q_done[(qc, tau)] = ("dv", emit_rope(dst, ev_evac))
                else:
                    q_done[(qc, tau)] = ("sc", ev_evac)
            if qc < 8 and qc + 1 < 8:
                prefetch_tabs(qc + 1)
            emit_wa_load()

        pe_after_q = [P.cnt["pe"]]

        P.wait("vector", "ar", ev_aro)
        ev_yq = None
        for c in range(NCD):
            for tau in (0, 1):
                s, v = q_done[(c, tau)]
                P.wait("vector", s, v)
            ev_yq = P.sig("vector", lambda e, c=c: e.tensor_scalar(
                q_sb[:, c * TPC:(c + 1) * TPC], q_sb[:, c * TPC:(c + 1) * TPC],
                aro_sb[:, c:c + 1], None, OP.mult), "dv")

        ln2_pend = []
        ln2_feed, ln2_st = make_ln_sink(
            ao_sb,
            {0: ("p_sum", p_sum[0:1, :]), 1: ("p_rh0", p_rh[0][0:1, :])},
            {0: ("p_sq", p_sq[0:1, :]), 1: ("p_rh1", p_rh[1][0:1, :])})
        ao_ready = {}
        rotO = rot([p_acc[0], p_acc[1], p_bc, p_inv])
        for li4 in range(16):
            li = 48 + li4
            e_ch = li4
            for tau in (0, 1):
                bank = rotO()
                P.wait_ev("tensor", wa_loads[li])
                if li4 == 0 and tau == 0:
                    P.wait("tensor", "dv", ev_yq)
                tiles = [wA_tile(li, c) for c in range(NCD)]
                rhs = [atk(q_sb, c, tau) for c in range(NCD)]
                ev_g = main_group(tiles, rhs, bank)
                if tau == 1:
                    wa_user_ev[li] = ("pe", ev_g)
                P.wait("vector", "pe", ev_g)
                ev_ao = P.sig("vector", lambda e, ec=e_ch, tau=tau, bank=bank:
                              e.scalar_tensor_tensor(
                                  atk(ao_sb, ec, tau), bank.ap(), bout_sb[:, ec:ec + 1],
                                  atk(xn_sb, ec, tau), OP.add, OP.add), "dv")
                bank_free[bank.name] = ("dv", ev_ao)
                ao_ready[(e_ch, tau)] = ev_ao
                ln2_pend.append((e_ch, tau, ("dv", ev_ao)))
                if len(ln2_pend) > 2:
                    ln2_feed(*ln2_pend.pop(0))
            emit_wa_load()
        while ln2_pend:
            ln2_feed(*ln2_pend.pop(0))

        ev_u = {}
        ev_xn2_t = {}

        u_rows = {0: (r_b(0), r_b(1)), 1: (r_b(2), r_b(3))}

        def emit_urows(tau):
            hi, lo = u_rows[tau]
            P.wait_ev("tensor", bank_free.get("p_sum"))
            P.wait("tensor", "dv", ev_xn2_t[tau])
            ev_us = None
            for c in range(NCD):
                last = c == NCD - 1
                fn = lambda e, c=c, st=(c == 0), sp=last, tau=tau: e.matmul(
                    p_sum[0:1, :], onc_sb[:], atk(xn_sb, c, tau), start=st, stop=sp)
                if last:
                    ev_us = P.sig("tensor", fn, "pe")
                else:
                    P.run("tensor", fn)
            P.wait("vector", "pe", ev_us)
            P.sig("vector", lambda e: e.tensor_scalar(
                r_f(0), p_sum[0:1, :], -128.0, None, OP.mult), "dv")
            bank_free["p_sum"] = ("dv", P.cnt["dv"])
            P.run("vector", lambda e, hi=hi: e.tensor_copy(hi, r_f(0)))
            P.run("vector", lambda e, hi=hi: e.tensor_sub(r_f(0), r_f(0), hi))
            ev_u[tau] = P.sig("vector", lambda e, lo=lo: e.tensor_copy(lo, r_f(0)), "dv")

        xn_guard = ("pe", pe_after_q[0])
        ev_ln2_0 = emit_ln_tail(("p_sum", p_sum[0:1, :]), ("p_sq", p_sq[0:1, :]),
                                ln2_st["sum_ev"][0], ln2_st["sq_ev"][0],
                                None, ao_sb, 0, resid=True)
        _, ev_sq3_0 = emit_stats_resid(0, skip_sum=True, xn_guard=xn_guard,
                                       src_sc_ev=("dv", ev_ln2_0))
        ev_ln2 = emit_ln_tail(("p_rh0", p_rh[0][0:1, :]), ("p_rh1", p_rh[1][0:1, :]),
                              ln2_st["sum_ev"][1], ln2_st["sq_ev"][1],
                              None, ao_sb, 1, resid=True)
        ev_xn2_t[0] = emit_ln_tail(("p_sum", p_sum[0:1, :]), ("p_sq", p_sq[0:1, :]),
                                   None, ev_sq3_0, x_sb, None, 0, resid=False,
                                   mu_from=mu_save(0))
        _, ev_sq3_1 = emit_stats_resid(1, skip_sum=True,
                                       src_sc_ev=("dv", ev_ln2))
        ev_xn2_t[1] = emit_ln_tail(("p_sum", p_sum[0:1, :]), ("p_sq", p_sq[0:1, :]),
                                   None, ev_sq3_1, x_sb, None, 1, resid=False,
                                   mu_from=mu_save(1))
        emit_urows(0)
        emit_urows(1)
        ev_xn2 = ev_xn2_t[1]

        wb_loads = []
        wb_user_ev = {}
        wb_state = [0]

        def emit_wb_load():
            i = wb_state[0]
            if i >= 2 * NCH:
                return
            if i >= 2:
                P.wait_ev("gpsimd", wb_user_ev.get(i - 2))
            h, which = divmod(i, 2)
            src = (wqg_r if which == 0 else wqu_r)[h]
            sem = f"wb{i % 2}"
            v = P.sig("gpsimd", lambda e, s=src, i=i: e.dma_start(out=wB_slot(i), in_=s),
                      sem, 16)
            wb_loads.append((sem, v))
            wb_state[0] += 1

        emit_wb_load()
        emit_wb_load()

        h_ready = {}
        rotG = rot([p_acc[0], p_acc[1], p_bc, p_inv])
        for h in range(NCH):
            evac_g = {}
            for tau in (0, 1):
                for which in (0, 1):
                    li = 2 * h + which
                    bank = rotG()
                    P.wait_ev("tensor", wb_loads[li])
                    if h == 0 and which == 0:
                        P.wait("tensor", "dv", ev_u[tau])
                    tiles = [wB_tile(li, c) for c in range(NCD)]
                    rhs = [atk(xn_sb, c, tau) for c in range(NCD)]
                    k1 = [(onr_sb[:], u_rows[tau][0]), (onr_sb[:], u_rows[tau][1])]
                    ev_g = main_group(tiles, rhs, bank, extra_k1=k1)
                    if tau == 1:
                        wb_user_ev[li] = ("pe", ev_g)
                    P.wait("scalar", "pe", ev_g)
                    if which == 0:
                        P.wait_ev("scalar", free.get(("g", tau)))
                        ev_e2 = P.sig("scalar", lambda e, h=h, tau=tau, bank=bank:
                                      e.activation(g_tmp[:, tau * NT:(tau + 1) * NT],
                                                   bank.ap(), AF.Silu,
                                                   scale=sg_sb[:, h:h + 1]), "sc")
                    else:
                        P.wait_ev("scalar", free.get(("u", tau)))
                        ev_e2 = P.sig("scalar", lambda e, h=h, tau=tau, bank=bank:
                                      e.activation(u_tmp[:, tau * NT:(tau + 1) * NT],
                                                   bank.ap(), AF.Copy,
                                                   scale=su_sb[:, h:h + 1]), "sc")
                    bank_free[bank.name] = ("sc", ev_e2)
                    evac_g[(which, tau)] = ev_e2
                P.wait("vector", "sc", evac_g[(1, tau)])
                h_ready[(h, tau)] = P.sig("vector", lambda e, h=h, tau=tau: e.tensor_mul(
                    atk(h_sb, h, tau), g_tmp[:, tau * NT:(tau + 1) * NT],
                    u_tmp[:, tau * NT:(tau + 1) * NT]), "dv")
                free[("g", tau)] = ("dv", h_ready[(h, tau)])
                free[("u", tau)] = ("dv", h_ready[(h, tau)])
            emit_wb_load()
            emit_wb_load()

        ev_gu_pe_done = ("pe", P.cnt["pe"])

        for tau in (0, 1):
            hi, lo = u_rows[tau]
            P.wait_ev("tensor", bank_free.get("p_sum"))
            ev_hs = None
            for h in range(NCH):
                P.wait("tensor", "dv", h_ready[(h, tau)])
                last = h == NCH - 1
                fn = lambda e, h=h, st=(h == 0), sp=last, tau=tau: e.matmul(
                    p_sum[0:1, :], onc_sb[:], atk(h_sb, h, tau), start=st, stop=sp)
                if last:
                    ev_hs = P.sig("tensor", fn, "pe")
                else:
                    P.run("tensor", fn)
            P.wait("vector", "pe", ev_hs)
            P.sig("vector", lambda e: e.tensor_scalar(
                r_f(0), p_sum[0:1, :], -128.0, None, OP.mult), "dv")
            bank_free["p_sum"] = ("dv", P.cnt["dv"])
            P.run("vector", lambda e, hi=hi: e.tensor_copy(hi, r_f(0)))
            P.run("vector", lambda e, hi=hi: e.tensor_sub(r_f(0), r_f(0), hi))
            ev_u[tau] = P.sig("vector", lambda e, lo=lo: e.tensor_copy(lo, r_f(0)), "dv")

        ln4_pend = []
        ln4_feed, ln4_st = make_ln_sink(
            xn_sb,
            {0: ("p_sum", p_sum[0:1, :]), 1: ("p_rh0", p_rh[0][0:1, :])},
            {0: ("p_sq", p_sq[0:1, :]), 1: ("p_rh1", p_rh[1][0:1, :])})
        wd_loads = []
        wd_user_ev = {}
        wd_state = [0]

        def emit_wd_load():
            i = wd_state[0]
            if i >= 64:
                return
            if i < 2:
                P.wait_ev("gpsimd", ev_gu_pe_done)
            else:
                P.wait_ev("gpsimd", wd_user_ev.get(i - 2))
            ec, q = divmod(i, 4)
            src = wqd_r[ec][:, q * 11 * 128:(q + 1) * 11 * 128]
            sem = f"wd{i % 2}"
            v = P.sig("gpsimd", lambda e, s=src, i=i: e.dma_start(out=wD_slot(i), in_=s),
                      sem, 16)
            wd_loads.append((sem, v))
            wd_state[0] += 1

        emit_wd_load()
        emit_wd_load()

        dn_ready = {}
        ev_xn2_free = ("pe", P.cnt["pe"])
        dn_pairs = [(p_acc[0], p_acc[1]), (p_bc, p_inv)]
        for ec in range(16):
            ev_stop = {}
            pair = dn_pairs[ec % 2]
            for q in range(4):
                i = 4 * ec + q
                P.wait_ev("tensor", wd_loads[i])
                if i == 0:
                    P.wait("tensor", "dv", max(ev_u[0], ev_u[1]))
                for tau in (0, 1):
                    bank = pair[tau]
                    if q == 0:
                        P.wait_ev("tensor", bank_free.get(bank.name))
                    for cc in range(11):
                        h = q * 11 + cc
                        is_last_read = (tau == 1 and cc == 10)
                        fn = lambda e, i=i, cc=cc, h=h, tau=tau, bank=bank, \
                                st=(q == 0 and cc == 0): \
                            e.matmul(bank.ap(), wD_tile(i, cc), atk(h_sb, h, tau),
                                     start=st, stop=False)
                        if is_last_read:
                            ev_qr = P.sig("tensor", fn, "pe")
                        else:
                            P.run("tensor", fn)
                wd_user_ev[i] = ("pe", ev_qr)
                emit_wd_load()
            for tau in (0, 1):
                bank = pair[tau]
                P.run("tensor", lambda e, r=u_rows[tau][0], bank=bank: e.matmul(
                    bank.ap(), onr_sb[:], r, start=False, stop=False))
                ev_stop[tau] = P.sig("tensor", lambda e, r=u_rows[tau][1], bank=bank: e.matmul(
                    bank.ap(), onr_sb[:], r, start=False, stop=True), "pe")
            for tau in (0, 1):
                bank = pair[tau]
                P.wait("scalar", "pe", ev_stop[tau])
                if ec == 0 and tau == 0:
                    P.wait_ev("scalar", ev_xn2_free)
                ev_dn = P.sig("scalar", lambda e, ec=ec, tau=tau, bank=bank: e.activation(
                    atk(xn_sb, ec, tau), bank.ap(), AF.Copy,
                    scale=sd_sb[:, ec:ec + 1]), "sc")
                bank_free[bank.name] = ("sc", ev_dn)
                dn_ready[(ec, tau)] = ev_dn
                ln4_pend.append((ec, tau, ("sc", ev_dn)))
                if len(ln4_pend) > 2:
                    ln4_feed(*ln4_pend.pop(0))

        while ln4_pend:
            ln4_feed(*ln4_pend.pop(0))

        def store_chunk(c, ev):
            P.wait("sync", "dv", ev)
            P.sig("sync", lambda e, c=c: e.dma_start(
                out=yT[c * 128:(c + 1) * 128, :],
                in_=x_sb[:, c * TPC:(c + 1) * TPC]), "st", 16)

        for tau in (0, 1):
            if tau == 0:
                sb, qb = ("p_sum", p_sum[0:1, :]), ("p_sq", p_sq[0:1, :])
            else:
                sb, qb = ("p_rh0", p_rh[0][0:1, :]), ("p_rh1", p_rh[1][0:1, :])
            emit_ln_tail(sb, qb, ln4_st["sum_ev"][tau], ln4_st["sq_ev"][tau],
                         None, xn_sb, tau, resid=True,
                         on_chunk=store_chunk if tau == 1 else None)
        P.wait("sync", "st", 16 * NCD)

        assert wa_state[0] == 64 and wb_state[0] == 2 * NCH and wd_state[0] == 64

        with nc.Block() as block:
            block.sync(lambda e: [f(e) for f in P.ops["sync"]])
            block.gpsimd(lambda e: [f(e) for f in P.ops["gpsimd"]])
            block.tensor(lambda e: [f(e) for f in P.ops["tensor"]])
            block.vector(lambda e: [f(e) for f in P.ops["vector"]])
            block.scalar(lambda e: [f(e) for f in P.ops["scalar"]])

    return nc


_BF = ml_dtypes.bfloat16


def _prep_shared(W_in, b_in, W_out, b_out, wq_gate, s_gate, wq_up, s_up,
                 wq_down, s_down):
    c = np.ascontiguousarray
    w_in_r = c(W_in.T.reshape(16, 128, 48, 128).transpose(2, 1, 0, 3).reshape(48, 128, 2048))
    w_out_r = c(W_out.T.reshape(16, 128, 16, 128).transpose(2, 1, 0, 3).reshape(16, 128, 2048))
    wqg_r = c(wq_gate.T.reshape(16, 128, 44, 128).transpose(2, 1, 0, 3).reshape(44, 128, 2048))
    wqu_r = c(wq_up.T.reshape(16, 128, 44, 128).transpose(2, 1, 0, 3).reshape(44, 128, 2048))
    wqd_r = c(wq_down.T.reshape(44, 128, 16, 128).transpose(2, 1, 0, 3).reshape(16, 128, 44 * 128))
    perm = np.zeros((128, 128), dtype=np.float32)
    for i in range(64):
        perm[2 * i, 2 * i + 1] = 1.0
        perm[2 * i + 1, 2 * i] = -1.0
    shared = dict(
        w_in_r=w_in_r, w_out_r=w_out_r, wqg_r=wqg_r, wqu_r=wqu_r, wqd_r=wqd_r,
        b_in_pp=c(b_in.reshape(48, 128).T), b_out_pp=c(b_out.reshape(16, 128).T),
        sg_pp=c(s_gate.reshape(44, 128).T), su_pp=c(s_up.reshape(44, 128).T),
        sd_pp=c(s_down.reshape(16, 128).T),
        ones_r=np.ones((1, 128), dtype=_BF), ones_c=np.ones((128, 1), dtype=_BF),
        perm=perm.astype(_BF))
    inv_freq = (1.0 / (10000.0 ** (np.arange(0, ROT, 2, dtype=np.float32) / ROT))).astype(np.float32)
    pos = np.arange(S, dtype=np.float32)
    freqs = np.repeat(pos[:, None] * inv_freq[None, :], 2, axis=-1)
    return shared, np.cos(freqs).astype(np.float32), np.sin(freqs).astype(np.float32)


def kernel(L0_attn_X, L0_attn_W_in, L0_attn_b_in, L0_attn_W_out, L0_attn_b_out,
           L0_ffn_wq_gate, L0_ffn_s_gate, L0_ffn_wq_up, L0_ffn_s_up,
           L0_ffn_wq_down, L0_ffn_s_down):
    X = np.asarray(L0_attn_X, dtype=np.float32)
    shared, cos_full, sin_full = _prep_shared(
        np.asarray(L0_attn_W_in, np.float32), np.asarray(L0_attn_b_in, np.float32),
        np.asarray(L0_attn_W_out, np.float32), np.asarray(L0_attn_b_out, np.float32),
        np.asarray(L0_ffn_wq_gate, np.int32), np.asarray(L0_ffn_s_gate, np.float32),
        np.asarray(L0_ffn_wq_up, np.int32), np.asarray(L0_ffn_s_up, np.float32),
        np.asarray(L0_ffn_wq_down, np.int32), np.asarray(L0_ffn_s_down, np.float32))

    nc = build_nc()
    in_maps = []
    for core in range(N_CORES):
        b, s0 = core // 4, (core % 4) * TPC
        m = dict(shared)
        m["xT"] = np.ascontiguousarray(X[b, s0:s0 + TPC, :].T)
        m["cosT"] = np.ascontiguousarray(cos_full[s0:s0 + TPC, :].T).astype(_BF)
        m["sinT"] = np.ascontiguousarray(sin_full[s0:s0 + TPC, :].T).astype(_BF)
        in_maps.append(m)

    res = run_bass_kernel_spmd(nc, in_maps, core_ids=list(range(N_CORES)))
    out = np.empty((B, S, D), dtype=np.float32)
    for core in range(N_CORES):
        b, s0 = core // 4, (core % 4) * TPC
        out[b, s0:s0 + TPC, :] = res.results[core]["yT"].T
    return out


if __name__ == "__main__":
    nc = build_nc()
    print("instructions:", nc.n_instructions())
